# version 04 (final)
# baseline (speedup 1.0000x reference)
import math
from contextlib import ExitStack

import numpy as np
import ml_dtypes

import concourse.bass as bass
import concourse.mybir as mybir
from concourse.bass_utils import run_bass_kernel_spmd
from concourse.alu_op_type import AluOpType as OP

F32 = mybir.dt.float32
BF16 = mybir.dt.bfloat16
I32 = mybir.dt.int32
AF = mybir.ActivationFunctionType
AX = mybir.AxisListType

NCORES = 8
S = 8192
D = 2048
TPC = S // NCORES
NTT = TPC // 128
NDC = D // 128
BIG = 1e30
M0 = 10.0


ENG_STREAMS = ("tensor", "vector", "scalar", "gpsimd", "sync")


class TR:

    def __init__(self):
        self.insts = []
        self.last_writer = {}
        self.readers = {}
        self.counts = {s: 0 for s in ENG_STREAMS}
        self.pending_phase = None

    def _dep_of_unit_for_read(self, u):
        return [self.last_writer[u]] if u in self.last_writer else []

    def _record(self, stream, kind, thunk, reads, writes, tag):
        idx = len(self.insts)
        if self.pending_phase is not None:
            ph, picked = self.pending_phase
            if stream not in picked:
                picked.add(stream)
                reads = list(reads) + [ph]
                if len(picked) == len(ENG_STREAMS):
                    self.pending_phase = None
        deps = set()
        for u in reads:
            for j in self._dep_of_unit_for_read(u):
                deps.add(j)
        for u in writes:
            for j in self._dep_of_unit_for_read(u):
                deps.add(j)
            for j in self.readers.get(u, ()):
                deps.add(j)
        inst = dict(idx=idx, stream=stream, kind=kind, thunk=thunk,
                    deps=sorted(deps), pos=self.counts[stream], tag=tag,
                    inc_needed=(kind != "eng"))
        self.insts.append(inst)
        self.counts[stream] += 1
        for u in reads:
            self.readers.setdefault(u, []).append(idx)
        for u in writes:
            self.last_writer[u] = idx
            self.readers[u] = []
        return inst

    def op(self, stream, thunk, reads=(), writes=(), tag=""):
        assert stream in ("tensor", "vector", "scalar", "gpsimd")
        return self._record(stream, "eng", thunk, reads, writes, tag)

    def dma(self, stream, thunk, reads=(), writes=(), tag=""):
        assert stream in ("sync", "gpsimd", "scalar", "vector")
        return self._record(stream, "dma", thunk, reads, writes, tag)

    def cc(self, thunk, reads=(), writes=(), tag=""):
        return self._record("gpsimd", "cc", thunk, reads, writes, tag)

    DMA_SLOTS = 6

    def finalize(self, nc, ctx):
        for inst in self.insts:
            for j in inst["deps"]:
                dep = self.insts[j]
                same_stream = dep["stream"] == inst["stream"]
                if dep["kind"] == "eng" and not same_stream:
                    dep["inc_needed"] = True
        engval = {s: 0 for s in ENG_STREAMS}
        dmaidx = {s: 0 for s in ENG_STREAMS}
        slot_uses = {}
        ccval = 0
        for inst in self.insts:
            s = inst["stream"]
            if inst["kind"] == "eng":
                if inst["inc_needed"]:
                    engval[s] += 1
                inst["semval"] = engval[s]
            elif inst["kind"] == "dma":
                slot = dmaidx[s] % self.DMA_SLOTS
                dmaidx[s] += 1
                key = (s, slot)
                slot_uses[key] = slot_uses.get(key, 0) + 1
                inst["slot"] = slot
                inst["semval"] = 16 * slot_uses[key]
            else:
                ccval += 1
                inst["semval"] = ccval

        sems = {}
        for s in ENG_STREAMS:
            if any(i["stream"] == s and i["kind"] == "eng" and i["inc_needed"]
                   for i in self.insts):
                sems[("eng", s)] = ctx.enter_context(nc.semaphore(f"se_{s}"))
        for (s, slot) in sorted(slot_uses.keys()):
            sems[("dma", s, slot)] = ctx.enter_context(nc.semaphore(f"sd_{s}{slot}"))
        if any(i["kind"] == "cc" for i in self.insts):
            sems[("cc",)] = ctx.enter_context(nc.semaphore("s_cc"))

        observed = {s: {} for s in ENG_STREAMS}
        proven = {s: -1 for s in ENG_STREAMS}
        for inst in self.insts:
            s = inst["stream"]
            waits = {}
            drain = False
            if inst["kind"] == "dma" and inst["semval"] > 16:
                waits[("dma", s, inst["slot"])] = inst["semval"] - 16
            for j in inst["deps"]:
                dep = self.insts[j]
                ds = dep["stream"]
                if dep["kind"] == "eng" and ds == s:
                    if s != "tensor" and dep["pos"] > proven[s]:
                        drain = True
                    continue
                if dep["kind"] == "cc":
                    semkey = ("cc",)
                elif dep["kind"] == "dma":
                    semkey = ("dma", ds, dep["slot"])
                else:
                    semkey = ("eng", ds)
                v = dep["semval"]
                if waits.get(semkey, 0) < v:
                    waits[semkey] = v
            final_waits = []
            for semkey, v in sorted(waits.items(), key=lambda kv: str(kv[0])):
                if observed[s].get(semkey, 0) >= v:
                    continue
                observed[s][semkey] = v
                final_waits.append((semkey, v))
            if drain:
                proven[s] = inst["pos"] - 1
            inst["waits"] = final_waits
            inst["drain"] = drain

        block = ctx.enter_context(nc.Block())
        tr = self

        def emit_stream(eng, stream_name):
            for inst in tr.insts:
                if inst["stream"] != stream_name:
                    continue
                if inst["drain"]:
                    eng.drain()
                for semkey, v in inst["waits"]:
                    eng.wait_ge(sems[semkey], v)
                bi = inst["thunk"](eng)
                if inst["kind"] == "eng" and inst["inc_needed"]:
                    bi.then_inc(sems[("eng", stream_name)], 1)
                elif inst["kind"] == "dma":
                    bi.then_inc(sems[("dma", stream_name, inst["slot"])], 16)
                elif inst["kind"] == "cc":
                    bi.then_inc(sems[("cc",)])

        @block.tensor
        def _(tensor):
            emit_stream(tensor, "tensor")

        @block.vector
        def _(vector):
            emit_stream(vector, "vector")

        @block.scalar
        def _(scalar):
            emit_stream(scalar, "scalar")

        @block.gpsimd
        def _(gpsimd):
            emit_stream(gpsimd, "gpsimd")

        @block.sync
        def _(sync):
            emit_stream(sync, "sync")


    def phase_barrier(self, name):
        ph = ("PHASE", name)
        deps = self._frontier()
        inst = self.op("gpsimd", lambda e: e.engine_nop(), reads=(), writes=[ph],
                       tag=f"phase_{name}")
        inst["deps"] = sorted(set(deps))
        self.pending_phase = (ph, {"gpsimd"})
        return ph

    def _frontier(self):
        last_eng = {}
        last_dma = {}
        last_cc = None
        dmaidx = {s: 0 for s in ENG_STREAMS}
        for i in self.insts:
            if i["kind"] == "eng":
                last_eng[i["stream"]] = i["idx"]
            elif i["kind"] == "dma":
                slot = dmaidx[i["stream"]] % self.DMA_SLOTS
                dmaidx[i["stream"]] += 1
                last_dma[(i["stream"], slot)] = i["idx"]
            else:
                last_cc = i["idx"]
        deps = list(last_eng.values()) + list(last_dma.values())
        if last_cc is not None:
            deps.append(last_cc)
        return deps

    def final_barrier(self):
        deps = self._frontier()
        inst = self.op("gpsimd", lambda e: e.engine_nop(), reads=(),
                       writes=(), tag="final_barrier")
        inst["deps"] = sorted(set(deps))
        return inst



def ceil_div(a, b):
    return (a + b) // b if a % b else a // b


class Prog:

    def __init__(self, name):
        self.nc = bass.Bass("TRN2", target_bir_lowering=False)
        self.ctx = ExitStack()
        self.tr = TR()
        self.sb = {}
        self.overlay = ExitStack()

    def sbuf(self, name, shape, dtype, scoped=False):
        stack = self.overlay if scoped else self.ctx
        side = "right" if scoped else "left"
        t = stack.enter_context(self.nc.sbuf_tensor(name, shape, dtype, side=side))
        self.sb[name] = t
        return t

    def psum(self, name, shape, dtype, scoped=True):
        stack = self.overlay if scoped else self.ctx
        side = "right" if scoped else "left"
        t = stack.enter_context(self.nc.psum_tensor(name, shape, dtype, side=side))
        self.sb[name] = t
        return t

    def new_phase(self, name):
        self.tr.phase_barrier(name)
        self.overlay.close()
        self.overlay = ExitStack()

    def const_col(self, value):
        key = ("constcol", float(value))
        if key in self.sb:
            return self.sb[key]
        t = self.ctx.enter_context(
            self.nc.sbuf_tensor(f"const_{len(self.sb)}", [128, 1], F32))
        self.sb[key] = t
        self.tr.op("gpsimd", (lambda t, value: lambda e: e.memset(t[:], float(value)))(t, value),
                   writes=[key], tag=f"const{value}")
        return t

    def din(self, name, shape, dtype):
        return self.nc.dram_tensor(name, shape, dtype, kind="ExternalInput").ap()

    def dout(self, name, shape, dtype):
        return self.nc.dram_tensor(name, shape, dtype, kind="ExternalOutput").ap()

    def dint(self, name, shape, dtype):
        return self.nc.dram_tensor(name, shape, dtype, kind="Internal")

    def finish(self):
        self.tr.final_barrier()
        self.tr.finalize(self.nc, self.ctx)
        self.overlay.close()
        self.ctx.close()
        return self.nc


def emit_ln(p, xt_unit, xt_ap, out_ap, out_unit, scr, tag=""):
    tr = p.tr
    su, negm, ssq, std, rstd = scr["su"], scr["negm"], scr["ssq"], scr["std"], scr["rstd"]
    squ = scr["squnit"]
    tr.op("vector", lambda e: e.tensor_reduce(
        out=su[:], in_=xt_ap, axis=AX.X, op=OP.add),
        reads=[xt_unit], writes=[squ + "su"], tag=tag + "ln_sum")
    tr.op("vector", lambda e: e.tensor_scalar(
        out=negm[:], in0=su[:], scalar1=-1.0 / D, scalar2=None, op0=OP.mult),
        reads=[squ + "su"], writes=[squ + "negm"], tag=tag + "ln_negm")
    tr.op("scalar", lambda e: e.activation(
        out=scr["sq"][:], in_=xt_ap, func=AF.Square, bias=negm[:], scale=1.0,
        accum_out=ssq[:]),
        reads=[xt_unit, squ + "negm"], writes=[squ + "sq", squ + "ssq"], tag=tag + "ln_sq")
    eps = p.const_col(1e-5)
    tr.op("scalar", lambda e: e.activation(
        out=std[:], in_=ssq[:], func=AF.Sqrt, bias=eps[:], scale=1.0 / D),
        reads=[squ + "ssq", ("constcol", 1e-5)], writes=[squ + "std"], tag=tag + "ln_std")
    tr.op("vector", lambda e: e.reciprocal(out=rstd[:], in_=std[:]),
        reads=[squ + "std"], writes=[squ + "rstd"], tag=tag + "ln_rstd")
    tr.op("vector", lambda e: e.tensor_scalar(
        out=out_ap, in0=xt_ap, scalar1=negm[:], scalar2=rstd[:],
        op0=OP.add, op1=OP.mult),
        reads=[xt_unit, squ + "negm", squ + "rstd"], writes=[out_unit], tag=tag + "ln_out")



def build_launch_A():
    p = Prog("A")
    nc, tr = p.nc, p.tr
    x_in = p.din("x_in", [TPC, D], F32)
    wq_in = p.din("wq", [D, D], F32)
    wk_in = p.din("wk", [D, D], F32)
    wv_in = p.din("wv", [D, D], F32)
    qT_out = p.dout("qT", [D, TPC], BF16)
    kT_out = p.dout("kT", [D, TPC], BF16)
    v_out = p.dout("v", [TPC, D], BF16)

    xt = [p.sbuf(f"xt{i}", [128, D], F32) for i in range(2)]
    xln = p.sbuf("xln", [128, D], BF16)
    xlnT = p.sbuf("xlnT", [128, NDC * TPC], BF16)
    w_buf = p.sbuf("w_buf", [128, NDC, D], BF16)
    w_sb = {"wq": w_buf, "wk": w_buf, "wv": w_buf}
    stage = [p.sbuf(f"stage{i}", [128, 512], BF16) for i in range(4)]
    scr = dict(su=p.sbuf("su", [128, 1], F32), negm=p.sbuf("negm", [128, 1], F32),
               ssq=p.sbuf("ssq", [128, 1], F32), std=p.sbuf("std", [128, 1], F32),
               rstd=p.sbuf("rstd", [128, 1], F32), sq=p.sbuf("sq", [128, D], BF16),
               squnit="lnscr")
    id_bf = p.sbuf("id_bf", [128, 128], BF16)
    ps_tr = [p.psum(f"ps_tr{i}", [128, 1024], BF16) for i in range(2)]
    ps_mm = [p.psum(f"ps_mm{i}", [128, 512], F32) for i in range(4)]

    tr.op("gpsimd", lambda e: e.memset(id_bf[:], 0.0), writes=["id"], tag="id0")
    tr.op("gpsimd", lambda e: e.affine_select(
        out=id_bf[:], in_=id_bf[:], compare_op=OP.not_equal, fill=1.0,
        base=0, pattern=[[-1, 128]], channel_multiplier=1),
        reads=["id"], writes=["id"], tag="id1")

    def load_w(nm, ap_in):
        tr.dma("gpsimd", (lambda ap_in: lambda e: e.dma_start(
            out=w_buf[:], in_=ap_in.rearrange("(a p) c -> p a c", p=128)))(ap_in),
            writes=[("w",)], tag=f"load_{nm}")

    for i in range(NTT):
        b = xt[i % 2]
        tr.dma("sync", (lambda b, i: lambda e: e.dma_start(
            out=b[:], in_=x_in[i * 128:(i + 1) * 128, :]))(b, i),
            writes=[("xt", i % 2)], tag=f"ldx{i}")
        emit_ln(p, ("xt", i % 2), b[:], xln[:], ("xln",), scr, tag=f"t{i}_")
        for half in range(2):
            pst = ps_tr[half]
            for jj in range(8):
                j = half * 8 + jj
                tr.op("tensor", (lambda pst, jj, j: lambda e: e.transpose(
                    pst[:, jj * 128:(jj + 1) * 128], xln[:, j * 128:(j + 1) * 128],
                    id_bf[:]))(pst, jj, j),
                    reads=[("xln",), "id"], writes=[("pstr", half)], tag=f"tr{i}_{j}")
            for jj in range(8):
                j = half * 8 + jj
                tr.op("vector", (lambda pst, jj, j, i: lambda e: e.tensor_copy(
                    xlnT[:, j * TPC + i * 128: j * TPC + (i + 1) * 128],
                    pst[:, jj * 128:(jj + 1) * 128]))(pst, jj, j, i),
                    reads=[("pstr", half)], writes=[("xlnT", i, j)], tag=f"cp{i}_{j}")

    def proj_fmajor(nm, out_hbm):
        for c in range(NDC):
            for s in range(2):
                ps = ps_mm[(c * 2 + s) % 4]
                psu = ("psmm", (c * 2 + s) % 4)
                for j in range(NDC):
                    tr.op("tensor", (lambda ps, c, s, j: lambda e: e.matmul(
                        ps[:], w_buf[:, j, c * 128:(c + 1) * 128],
                        xlnT[:, j * TPC + s * 512: j * TPC + (s + 1) * 512],
                        start=(j == 0), stop=(j == NDC - 1)))(ps, c, s, j),
                        reads=[("w",)] + [("xlnT", i, j) for i in range(s * 4, s * 4 + 4)],
                        writes=[psu], tag=f"{nm}mm{c}_{s}_{j}")
                st = stage[(c * 2 + s) % 4]
                stu = ("stage", (c * 2 + s) % 4)
                tr.op("vector", (lambda ps, st: lambda e: e.tensor_copy(st[:], ps[:]))(ps, st),
                    reads=[psu], writes=[stu], tag=f"{nm}ev{c}_{s}")
                tr.dma("sync", (lambda st, out_hbm, c, s: lambda e: e.dma_start(
                    out=out_hbm[c * 128:(c + 1) * 128, s * 512:(s + 1) * 512],
                    in_=st[:]))(st, out_hbm, c, s),
                    reads=[stu], tag=f"{nm}st{c}_{s}")

    def proj_tmajor(nm, out_hbm):
        for i in range(NTT):
            for c4 in range(4):
                ps = ps_mm[(i * 4 + c4) % 4]
                psu = ("psmm", (i * 4 + c4) % 4)
                for j in range(NDC):
                    tr.op("tensor", (lambda ps, i, c4, j: lambda e: e.matmul(
                        ps[:], xlnT[:, j * TPC + i * 128: j * TPC + (i + 1) * 128],
                        w_buf[:, j, c4 * 512:(c4 + 1) * 512],
                        start=(j == 0), stop=(j == NDC - 1)))(ps, i, c4, j),
                        reads=[("w",), ("xlnT", i, j)],
                        writes=[psu], tag=f"{nm}mm{i}_{c4}_{j}")
                st = stage[(i * 4 + c4) % 4]
                stu = ("stage", (i * 4 + c4) % 4)
                tr.op("vector", (lambda ps, st: lambda e: e.tensor_copy(st[:], ps[:]))(ps, st),
                    reads=[psu], writes=[stu], tag=f"{nm}ev{i}_{c4}")
                tr.dma("sync", (lambda st, i, c4: lambda e: e.dma_start(
                    out=out_hbm[i * 128:(i + 1) * 128, c4 * 512:(c4 + 1) * 512],
                    in_=st[:]))(st, i, c4),
                    reads=[stu], tag=f"{nm}st{i}_{c4}")

    load_w("wq", wq_in)
    proj_fmajor("wq", qT_out)
    load_w("wk", wk_in)
    proj_fmajor("wk", kT_out)
    load_w("wv", wv_in)
    proj_tmajor("wv", v_out)

    return p.finish()



NKT = 72


def build_launch_B():
    p = Prog("B")
    nc, tr = p.nc, p.tr
    x_in = p.din("x_in", [TPC, D], F32)
    kT_ext = p.din("kT_ext", [D, NKT * 128], BF16)
    v_ext = p.din("v_ext", [16, 128, NKT * 128], BF16)
    qT_l = p.din("qT_l", [16, 128, TPC], BF16)
    wo_l = p.din("wo_l", [16, 128, NDC, 128], F32)
    bo_l = p.din("bo_l", [128, NDC], F32)
    bias_tbl_in = p.din("bias_tbl", [128, 16 * NKT], F32)
    scale_tbl_in = p.din("scale_tbl", [128, NKT], F32)
    x_out = p.dout("x_out", [TPC, D], F32)

    linv_dram = p.dint("linv_dram", [1, TPC], F32)
    ctx_all = p.sbuf("ctx_all", [128, 16 * TPC], BF16)
    bias_tbl = p.sbuf("bias_tbl_sb", [128, 16 * NKT], F32)
    scale_tbl = p.sbuf("scale_tbl_sb", [128, NKT], F32)
    tri = p.sbuf("tri", [128, 128], F32)
    id_bf = p.sbuf("id_bf", [128, 128], BF16)
    ones_bf = p.sbuf("ones_bf", [128, 1], BF16)
    linv = p.sbuf("linv", [1, TPC], F32)
    S_sb = p.sbuf("S_sb", [128, TPC], F32)

    tr.op("gpsimd", lambda e: e.memset(id_bf[:], 0.0), writes=["id"], tag="id0")
    tr.op("gpsimd", lambda e: e.affine_select(
        out=id_bf[:], in_=id_bf[:], compare_op=OP.not_equal, fill=1.0,
        base=0, pattern=[[-1, 128]], channel_multiplier=1),
        reads=["id"], writes=["id"], tag="id1")
    tr.op("gpsimd", lambda e: e.memset(tri[:], 0.0), writes=["tri"], tag="tri0")
    tr.op("gpsimd", lambda e: e.affine_select(
        out=tri[:], in_=tri[:], compare_op=OP.is_ge, fill=-BIG,
        base=0, pattern=[[1, 128]], channel_multiplier=-1),
        reads=["tri"], writes=["tri"], tag="tri1")
    tr.op("gpsimd", lambda e: e.memset(ones_bf[:], 1.0), writes=["ones"], tag="ones")
    tr.dma("sync", lambda e: e.dma_start(out=bias_tbl[:], in_=bias_tbl_in),
           writes=["btbl"], tag="ld_btbl")
    tr.dma("sync", lambda e: e.dma_start(out=scale_tbl[:], in_=scale_tbl_in),
           writes=["stbl"], tag="ld_stbl")

    kh = [p.sbuf(f"kh{i}", [128, NKT * 128], BF16, scoped=True) for i in range(2)]
    vh = [p.sbuf(f"vh{i}", [128, NKT * 128], BF16, scoped=True) for i in range(2)]
    qh = [p.sbuf(f"qh{i}", [128, TPC], BF16, scoped=True) for i in range(2)]
    ptile = [p.sbuf(f"pt{i}", [128, TPC], BF16, scoped=True) for i in range(3)]
    ps_zz = p.psum("ps_zz", [128, 2048], F32)
    ps_ctx = p.psum("ps_ctx", [128, TPC], F32)
    ps_l = p.psum("ps_l", [128, TPC], F32)

    for h in range(16):
        hb = h % 2
        tr.dma("sync", (lambda h, hb: lambda e: e.dma_start(
            out=kh[hb][:], in_=kT_ext[h * 128:(h + 1) * 128, :]))(h, hb),
            writes=[("kh", hb)], tag=f"ldk{h}")
        tr.dma("sync", (lambda h, hb: lambda e: e.dma_start(
            out=vh[hb][:], in_=v_ext[h, :, :]))(h, hb),
            writes=[("vh", hb)], tag=f"ldv{h}")
        tr.dma("sync", (lambda h, hb: lambda e: e.dma_start(
            out=qh[hb][:], in_=qT_l[h, :, :]))(h, hb),
            writes=[("qh", hb)], tag=f"ldq{h}")
        for kt in range(NKT):
            zz = ps_zz
            zof = (kt % 2) * 1024
            zu = ("zz", kt % 2)
            pt = ptile[kt % 3]
            ptu = ("pt", kt % 3)
            for s in range(2):
                tr.op("tensor", (lambda h, hb, kt, s, zof: lambda e: e.matmul(
                    ps_zz[:, zof + s * 512: zof + (s + 1) * 512],
                    kh[hb][:, kt * 128:(kt + 1) * 128],
                    qh[hb][:, s * 512:(s + 1) * 512],
                    start=True, stop=True))(h, hb, kt, s, zof),
                    reads=[("kh", hb), ("qh", hb)], writes=[zu], tag=f"z{h}_{kt}_{s}")
            c0 = 0
            if kt >= 64:
                b = kt - 64
                c0 = b * 128
                tr.op("vector", (lambda zof, c0: lambda e: e.tensor_tensor(
                    out=ps_zz[:, zof + c0: zof + c0 + 128],
                    in0=ps_zz[:, zof + c0: zof + c0 + 128], in1=tri[:],
                    op=OP.add))(zof, c0),
                    reads=[zu, "tri"], writes=[zu], tag=f"tri{h}_{kt}")
            tr.op("scalar", (lambda h, kt, zof, c0, pt: lambda e: e.activation(
                out=pt[:, c0:TPC], in_=ps_zz[:, zof + c0: zof + TPC], func=AF.Exp,
                bias=bias_tbl[:, h * NKT + kt: h * NKT + kt + 1],
                scale=scale_tbl[:, kt:kt + 1]))(h, kt, zof, c0, pt),
                reads=[zu, "btbl", "stbl"], writes=[ptu], tag=f"exp{h}_{kt}")
            if c0 > 0:
                tr.op("vector", (lambda pt, c0: lambda e: e.memset(
                    pt[:, 0:c0], 0.0))(pt, c0),
                    reads=[], writes=[ptu], tag=f"pz{h}_{kt}")
            for s in range(2):
                tr.op("tensor", (lambda hb, kt, s, pt: lambda e: e.matmul(
                    ps_ctx[:, s * 512:(s + 1) * 512],
                    vh[hb][:, kt * 128:(kt + 1) * 128],
                    pt[:, s * 512:(s + 1) * 512],
                    start=(kt == 0), stop=(kt == NKT - 1)))(hb, kt, s, pt),
                    reads=[("vh", hb), ptu], writes=[("ctxps",)], tag=f"ctx{h}_{kt}_{s}")
                tr.op("tensor", (lambda kt, s, pt: lambda e: e.matmul(
                    ps_l[0:1, s * 512:(s + 1) * 512], ones_bf[:],
                    pt[:, s * 512:(s + 1) * 512],
                    start=(kt == 0), stop=(kt == NKT - 1)))(kt, s, pt),
                    reads=["ones", ptu], writes=[("lps",)], tag=f"l{h}_{kt}_{s}")
        tr.op("vector", lambda e: e.reciprocal(out=linv[0:1, :], in_=ps_l[0:1, :]),
              reads=[("lps",)], writes=["linv"], tag=f"linv{h}")
        tr.dma("gpsimd", lambda e: e.dma_start(
            out=linv_dram.ap(), in_=linv[0:1, :]),
            reads=["linv"], writes=["linvd"], tag=f"linvd{h}")
        tr.dma("gpsimd", lambda e: e.dma_start(
            out=S_sb[:], in_=linv_dram.ap()[0:1, :].to_broadcast([128, TPC])),
            reads=["linvd"], writes=["S"], tag=f"bcast{h}")
        tr.op("vector", (lambda h: lambda e: e.tensor_tensor(
            out=ctx_all[:, h * TPC:(h + 1) * TPC], in0=ps_ctx[:],
            in1=S_sb[:], op=OP.mult))(h,),
            reads=[("ctxps",), "S"], writes=[("ctx", h)], tag=f"ev{h}")

    p.new_phase("oproj")
    xt = [p.sbuf(f"xt{i}", [128, D], F32, scoped=True) for i in range(NTT)]
    wob = [p.sbuf(f"wob{i}", [128, NDC, 128], BF16, scoped=True) for i in range(2)]
    stage = [p.sbuf(f"stg{i}", [128, 512], BF16, scoped=True) for i in range(2)]
    bo_sb = p.sbuf("bo_sb", [128, NDC], F32, scoped=True)
    ps_o = p.psum("ps_o", [128, 1024], F32)
    ps_tr = p.psum("ps_tr", [128, 1024], BF16)

    tr.dma("sync", lambda e: e.dma_start(out=bo_sb[:], in_=bo_l),
           writes=["bo"], tag="ld_bo")
    for i in range(NTT):
        tr.dma("sync", (lambda i: lambda e: e.dma_start(
            out=xt[i][:], in_=x_in[i * 128:(i + 1) * 128, :]))(i),
            writes=[("x", i)], tag=f"ldx{i}")
    for c in range(NDC):
        cb = c % 2
        tr.dma("gpsimd", (lambda c, cb: lambda e: e.dma_start(
            out=wob[cb][:], in_=wo_l[c, :, :, :]))(c, cb),
            writes=[("wo", cb)], tag=f"ldwo{c}")
        for s in range(2):
            po = (c * 2 + s) % 2
            pou = ("pso", po)
            for j in range(NDC):
                tr.op("tensor", (lambda c, cb, s, j, po: lambda e: e.matmul(
                    ps_o[:, po * 512:(po + 1) * 512], wob[cb][:, j, :],
                    ctx_all[:, j * TPC + s * 512: j * TPC + (s + 1) * 512],
                    start=(j == 0), stop=(j == NDC - 1)))(c, cb, s, j, po),
                    reads=[("wo", cb), ("ctx", j)], writes=[pou], tag=f"o{c}_{s}_{j}")
            st = stage[po]
            tr.op("vector", (lambda po, st, c: lambda e: e.tensor_scalar(
                out=st[:], in0=ps_o[:, po * 512:(po + 1) * 512],
                scalar1=bo_sb[:, c:c + 1], scalar2=None, op0=OP.add))(po, st, c),
                reads=[pou, "bo"], writes=[("stg", po)], tag=f"bo{c}_{s}")
            for xx in range(4):
                tr.op("tensor", (lambda st, xx, po: lambda e: e.transpose(
                    ps_tr[:, po * 512 + xx * 128: po * 512 + (xx + 1) * 128],
                    st[:, xx * 128:(xx + 1) * 128], id_bf[:]))(st, xx, po),
                    reads=[("stg", po), "id"], writes=[("pstr",)],
                    tag=f"otr{c}_{s}_{xx}")
            for xx in range(4):
                i = s * 4 + xx
                tr.op("vector", (lambda i, c, xx, po: lambda e: e.tensor_tensor(
                    out=xt[i][:, c * 128:(c + 1) * 128],
                    in0=ps_tr[:, po * 512 + xx * 128: po * 512 + (xx + 1) * 128],
                    in1=xt[i][:, c * 128:(c + 1) * 128], op=OP.add))(i, c, xx, po),
                    reads=[("pstr",), ("x", i)], writes=[("x", i)],
                    tag=f"acc{c}_{s}_{xx}")
    for i in range(NTT):
        tr.dma("sync", (lambda i: lambda e: e.dma_start(
            out=x_out[i * 128:(i + 1) * 128, :], in_=xt[i][:]))(i),
            reads=[("x", i)], tag=f"stx{i}")

    return p.finish()


def l0_attn_tables(c):
    Q0 = c * TPC
    cshift = Q0 + TPC - 1
    s_ = 1.0 / math.sqrt(128.0)
    slopes = 2.0 ** (-8.0 * np.arange(1, 17) / 16.0) * s_
    pp = np.arange(128)
    scale = np.zeros((128, NKT), np.float32)
    bias = np.zeros((128, 16 * NKT), np.float32)
    for kt in range(NKT):
        if kt < 64:
            k_abs = kt * 128 + pp
            valid = (kt * 128 + 127) < Q0
        else:
            k_abs = Q0 + (kt - 64) * 128 + pp
            valid = True
        scale[:, kt] = s_ if valid else 0.0
        for h in range(16):
            if valid:
                bias[:, h * NKT + kt] = -M0 + slopes[h] * (k_abs - cshift)
            else:
                bias[:, h * NKT + kt] = -BIG
    return scale, bias



HID = 5632
NHC = HID // 128
NSUP = 4
SUPC = NHC // NSUP


def emit_ln_and_transpose(p, x_in, xt, xlnT, scr, id_unit, id_bf, ps_tr, xln, tag):
    tr = p.tr
    for i in range(NTT):
        tr.dma("sync", (lambda i: lambda e: e.dma_start(
            out=xt[i][:], in_=x_in[i * 128:(i + 1) * 128, :]))(i),
            writes=[("x", i)], tag=f"{tag}ldx{i}")
        emit_ln(p, ("x", i), xt[i][:], xln[:], ("xln",), scr, tag=f"{tag}t{i}_")
        for half in range(2):
            pst = ps_tr
            pof = half * 1024
            for jj in range(8):
                j = half * 8 + jj
                tr.op("tensor", (lambda pof, jj, j: lambda e: e.transpose(
                    ps_tr[:, pof + jj * 128: pof + (jj + 1) * 128],
                    xln[:, j * 128:(j + 1) * 128], id_bf[:]))(pof, jj, j),
                    reads=[("xln",), id_unit], writes=[("ptrw", half)],
                    tag=f"{tag}tr{i}_{j}")
            for jj in range(8):
                j = half * 8 + jj
                tr.op("vector", (lambda pof, jj, j, i: lambda e: e.tensor_copy(
                    xlnT[:, j * TPC + i * 128: j * TPC + (i + 1) * 128],
                    ps_tr[:, pof + jj * 128: pof + (jj + 1) * 128]))(pof, jj, j, i),
                    reads=[("ptrw", half)], writes=[("xlnT", i, j)],
                    tag=f"{tag}cp{i}_{j}")


def build_launch_C():
    p = Prog("C")
    nc, tr = p.nc, p.tr
    x_in = p.din("x_in", [TPC, D], F32)
    wg_in = p.din("wg", [HID, D], I32)
    wu_in = p.din("wu", [HID, D], I32)
    wd_in = p.din("wd", [D, HID], I32)
    sg_in = p.din("sg_l", [128, NHC], F32)
    su_in = p.din("su_l", [128, NHC], F32)
    sd_in = p.din("sd_l", [128, NDC], F32)
    x_out = p.dout("x_out", [TPC, D], F32)

    xt = [p.sbuf(f"xt{i}", [128, D], F32) for i in range(NTT)]
    xln = p.sbuf("xln", [128, D], BF16)
    xlnT = p.sbuf("xlnT", [128, NDC * TPC], BF16)
    id_bf = p.sbuf("id_bf", [128, 128], BF16)
    scr = dict(su=p.sbuf("su", [128, 1], F32), negm=p.sbuf("negm", [128, 1], F32),
               ssq=p.sbuf("ssq", [128, 1], F32), std=p.sbuf("std", [128, 1], F32),
               rstd=p.sbuf("rstd", [128, 1], F32), sq=p.sbuf("sq", [128, D], BF16),
               squnit="lnscr")
    sg_sb = p.sbuf("sg_sb", [128, NHC], F32)
    su_sb = p.sbuf("su_sb", [128, NHC], F32)
    sd_sb = p.sbuf("sd_sb", [128, NDC], F32)
    inter = p.sbuf("inter", [128, SUPC * TPC], BF16)

    tr.op("gpsimd", lambda e: e.memset(id_bf[:], 0.0), writes=["id"], tag="id0")
    tr.op("gpsimd", lambda e: e.affine_select(
        out=id_bf[:], in_=id_bf[:], compare_op=OP.not_equal, fill=1.0,
        base=0, pattern=[[-1, 128]], channel_multiplier=1),
        reads=["id"], writes=["id"], tag="id1")
    for nm, dst, src in (("sg", sg_sb, sg_in), ("su", su_sb, su_in), ("sd", sd_sb, sd_in)):
        tr.dma("sync", (lambda dst, src: lambda e: e.dma_start(out=dst[:], in_=src))(dst, src),
               writes=[(nm,)], tag=f"ld{nm}")

    ps_tr = p.psum("ps_tr", [128, 2048], BF16)
    ps_g = p.psum("ps_g", [128, TPC], F32)
    ps_u = p.psum("ps_u", [128, TPC], F32)
    ps_d = p.psum("ps_d", [128, 512], F32)

    emit_ln_and_transpose(p, x_in, xt, xlnT, scr, "id", id_bf, ps_tr, xln, "C")

    wi = [p.sbuf(f"wi{i}", [128, D], I32) for i in range(1)]
    wf = p.sbuf("wf", [128, D], BF16)
    wT = [p.sbuf(f"wT{i}", [128, NDC * 128], BF16) for i in range(2)]
    gs = p.sbuf("gs", [128, TPC], BF16)
    wdi = [p.sbuf(f"wdi{i}", [128, SUPC * 128], I32) for i in range(2)]
    wdf = p.sbuf("wdf", [128, SUPC * 128], BF16)
    wdT = p.sbuf("wdT", [128, SUPC * D], BF16)

    def dq_transpose(src_ap, scol_ap, s128col_ap, sunits, dst, dstu, nt, tag):
        wib = wi[0]
        tr.dma("gpsimd", (lambda wib, src_ap: lambda e: e.dma_start(
            out=wib[:, 0:src_ap.shape[1]], in_=src_ap))(wib, src_ap),
            writes=[("wi", 0)], tag=f"{tag}ld")
        tr.op("vector", (lambda wib, scol_ap, s128col_ap, w: lambda e: e.tensor_scalar(
            out=wf[:, 0:w], in0=wib[:, 0:w], scalar1=scol_ap, scalar2=s128col_ap,
            op0=OP.mult, op1=OP.subtract))(wib, scol_ap, s128col_ap, 128 * nt),
            reads=[("wi", 0)] + list(sunits), writes=[("wf",)], tag=f"{tag}dq")
        for j in range(nt):
            tr.op("tensor", (lambda j: lambda e: e.transpose(
                ps_tr[:, (j % 16) * 128:(j % 16 + 1) * 128],
                wf[:, j * 128:(j + 1) * 128], id_bf[:]))(j),
                reads=[("wf",), "id"], writes=[("ptrw", (j % 16) // 8)], tag=f"{tag}tr{j}")
        for j in range(nt):
            tr.op("vector", (lambda j, dst: lambda e: e.tensor_copy(
                dst[:, j * 128:(j + 1) * 128],
                ps_tr[:, (j % 16) * 128:(j % 16 + 1) * 128]))(j, dst),
                reads=[("ptrw", (j % 16) // 8)], writes=[dstu], tag=f"{tag}cp{j}")


    sg128 = p.sbuf("sg128", [128, NHC], F32)
    su128 = p.sbuf("su128", [128, NHC], F32)
    sd128 = p.sbuf("sd128", [128, NDC], F32)
    for nm, dst, src in (("sg", sg128, sg_sb), ("su", su128, su_sb), ("sd", sd128, sd_sb)):
        tr.op("vector", (lambda dst, src: lambda e: e.tensor_scalar(
            out=dst[:], in0=src[:], scalar1=128.0, scalar2=None, op0=OP.mult))(dst, src),
            reads=[(nm,)], writes=[(nm + "128",)], tag=f"{nm}128")

    for sup in range(NSUP):
        for ci in range(SUPC):
            hc = sup * SUPC + ci
            bs = hc % 2
            dq_transpose(wg_in[hc * 128:(hc + 1) * 128, :], sg_sb[:, hc:hc + 1],
                         sg128[:, hc:hc + 1], [("sg",), ("sg128",)],
                         wT[0], ("wT", 0), NDC, f"g{hc}")
            for s in range(2):
                for j in range(NDC):
                    tr.op("tensor", (lambda s, j: lambda e: e.matmul(
                        ps_g[:, s * 512:(s + 1) * 512], wT[0][:, j * 128:(j + 1) * 128],
                        xlnT[:, j * TPC + s * 512: j * TPC + (s + 1) * 512],
                        start=(j == 0), stop=(j == NDC - 1)))(s, j),
                        reads=[("wT", 0)] + [("xlnT", i, j) for i in range(s * 4, s * 4 + 4)],
                        writes=[("psg",)], tag=f"gmm{hc}_{s}_{j}")
            tr.op("scalar", lambda e: e.activation(
                out=gs[:], in_=ps_g[:], func=AF.Silu),
                reads=[("psg",)], writes=[("gs",)], tag=f"silu{hc}")
            dq_transpose(wu_in[hc * 128:(hc + 1) * 128, :], su_sb[:, hc:hc + 1],
                         su128[:, hc:hc + 1], [("su",), ("su128",)],
                         wT[1], ("wT", 1), NDC, f"u{hc}")
            for s in range(2):
                for j in range(NDC):
                    tr.op("tensor", (lambda s, j: lambda e: e.matmul(
                        ps_u[:, s * 512:(s + 1) * 512], wT[1][:, j * 128:(j + 1) * 128],
                        xlnT[:, j * TPC + s * 512: j * TPC + (s + 1) * 512],
                        start=(j == 0), stop=(j == NDC - 1)))(s, j),
                        reads=[("wT", 1)] + [("xlnT", i, j) for i in range(s * 4, s * 4 + 4)],
                        writes=[("psu",)], tag=f"umm{hc}_{s}_{j}")
            tr.op("vector", (lambda ci: lambda e: e.tensor_tensor(
                out=inter[:, ci * TPC:(ci + 1) * TPC], in0=ps_u[:], in1=gs[:],
                op=OP.mult))(ci),
                reads=[("psu",), ("gs",)], writes=[("int", ci)], tag=f"int{hc}")
        for j in range(NDC):
            bs = j % 2
            tr.dma("gpsimd", (lambda j, bs, sup: lambda e: e.dma_start(
                out=wdi[bs][:], in_=wd_in[j * 128:(j + 1) * 128,
                                         sup * SUPC * 128:(sup + 1) * SUPC * 128]))(j, bs, sup),
                writes=[("wdi", bs)], tag=f"d{sup}_{j}ld")
            tr.op("vector", (lambda j, bs: lambda e: e.tensor_scalar(
                out=wdf[:], in0=wdi[bs][:], scalar1=sd_sb[:, j:j + 1],
                scalar2=sd128[:, j:j + 1], op0=OP.mult, op1=OP.subtract))(j, bs),
                reads=[("wdi", bs), ("sd",), ("sd128",)], writes=[("wdf",)],
                tag=f"d{sup}_{j}dq")
            for ci in range(SUPC):
                tr.op("tensor", (lambda j, ci: lambda e: e.transpose(
                    ps_tr[:, (ci % 16) * 128:(ci % 16 + 1) * 128],
                    wdf[:, ci * 128:(ci + 1) * 128], id_bf[:]))(j, ci),
                    reads=[("wdf",), "id"], writes=[("ptrw", (ci % 16) // 8)],
                    tag=f"d{sup}_{j}tr{ci}")
            for ci in range(SUPC):
                tr.op("vector", (lambda j, ci: lambda e: e.tensor_copy(
                    wdT[:, ci * D + j * 128: ci * D + (j + 1) * 128],
                    ps_tr[:, (ci % 16) * 128:(ci % 16 + 1) * 128]))(j, ci),
                    reads=[("ptrw", (ci % 16) // 8)], writes=[("wdT", ci, j)],
                    tag=f"d{sup}_{j}cp{ci}")
        for i in range(NTT):
            for c4 in range(4):
                for ci in range(SUPC):
                    tr.op("tensor", (lambda i, c4, ci: lambda e: e.matmul(
                        ps_d[:], inter[:, ci * TPC + i * 128: ci * TPC + (i + 1) * 128],
                        wdT[:, ci * D + c4 * 512: ci * D + (c4 + 1) * 512],
                        start=(ci == 0), stop=(ci == SUPC - 1)))(i, c4, ci),
                        reads=[("int", ci)] + [("wdT", ci, j) for j in range(c4 * 4, c4 * 4 + 4)],
                        writes=[("psd",)], tag=f"dmm{sup}_{i}_{c4}_{ci}")
                tr.op("vector", (lambda i, c4: lambda e: e.tensor_tensor(
                    out=xt[i][:, c4 * 512:(c4 + 1) * 512], in0=ps_d[:],
                    in1=xt[i][:, c4 * 512:(c4 + 1) * 512], op=OP.add))(i, c4),
                    reads=[("psd",), ("x", i)], writes=[("x", i)],
                    tag=f"dacc{sup}_{i}_{c4}")

    for i in range(NTT):
        tr.dma("sync", (lambda i: lambda e: e.dma_start(
            out=x_out[i * 128:(i + 1) * 128, :], in_=xt[i][:]))(i),
            reads=[("x", i)], tag=f"stx{i}")

    return p.finish()



MAGIC = 12582912.0


def emit_fp4(p, eng, src_unit, src_ap, dst_unit, dst_ap, q, W, tag):
    tr = p.tr
    NB = W // 16
    y, u1, u2, u3, u4 = q["y"], q["u1"], q["u2"], q["u3"], q["u4"]
    amax, scl, inv = q["amax"], q["scl"], q["inv"]
    uq = q["uq"]
    src_units = src_unit if isinstance(src_unit, list) else [src_unit]
    r3 = lambda ap: ap.rearrange("p (b k) -> p b k", k=16)
    tr.op("vector", lambda e: e.tensor_reduce(
        out=amax[:, 0:NB], in_=r3(src_ap), axis=AX.X, op=OP.max,
        apply_absolute_value=True),
        reads=src_units, writes=[uq + "amax"], tag=tag + "amax")
    tr.op("vector", lambda e: e.tensor_scalar(
        out=scl[:, 0:NB], in0=amax[:, 0:NB], scalar1=1e-37, scalar2=1.0 / 6.0,
        op0=OP.max, op1=OP.mult),
        reads=[uq + "amax"], writes=[uq + "scl"], tag=tag + "scl")
    tr.op("vector", lambda e: e.reciprocal(out=inv[:, 0:NB], in_=scl[:, 0:NB]),
        reads=[uq + "scl"], writes=[uq + "inv"], tag=tag + "inv")
    tr.op(eng, lambda e: e.tensor_tensor(
        out=r3(y[:, 0:W]), in0=r3(src_ap),
        in1=inv[:, 0:NB].unsqueeze(2).to_broadcast([128, NB, 16]), op=OP.mult),
        reads=src_units + [uq + "inv"], writes=[uq + "y"], tag=tag + "y")
    if eng == "gpsimd":
        tr.op(eng, lambda e: e.tensor_scalar(
            out=u1[:, 0:W], in0=y[:, 0:W], scalar1=-1.0, scalar2=None, op0=OP.mult),
            reads=[uq + "y"], writes=[uq + "u1"], tag=tag + "absneg")
        tr.op(eng, lambda e: e.tensor_tensor(
            out=u1[:, 0:W], in0=u1[:, 0:W], in1=y[:, 0:W], op=OP.max),
            reads=[uq + "y", uq + "u1"], writes=[uq + "u1"], tag=tag + "abs")
    else:
        tr.op(eng, lambda e: e.scalar_tensor_tensor(
            out=u1[:, 0:W], in0=y[:, 0:W], scalar=-1.0, in1=y[:, 0:W],
            op0=OP.mult, op1=OP.max),
            reads=[uq + "y"], writes=[uq + "u1"], tag=tag + "abs")
    tr.op(eng, lambda e: e.tensor_scalar(
        out=u2[:, 0:W], in0=u1[:, 0:W], scalar1=2.0, scalar2=None, op0=OP.is_ge),
        reads=[uq + "u1"], writes=[uq + "u2"], tag=tag + "t1")
    tr.op(eng, lambda e: e.tensor_scalar(
        out=u3[:, 0:W], in0=u1[:, 0:W], scalar1=4.0, scalar2=None, op0=OP.is_ge),
        reads=[uq + "u1"], writes=[uq + "u3"], tag=tag + "t2")
    tr.op(eng, lambda e: e.tensor_scalar(
        out=u1[:, 0:W], in0=u2[:, 0:W], scalar1=-2.0, scalar2=None, op0=OP.add),
        reads=[uq + "u2"], writes=[uq + "u1"], tag=tag + "a1")
    tr.op(eng, lambda e: e.tensor_scalar(
        out=u4[:, 0:W], in0=u3[:, 0:W], scalar1=0.5, scalar2=-1.0,
        op0=OP.mult, op1=OP.add),
        reads=[uq + "u3"], writes=[uq + "u4"], tag=tag + "a2")
    tr.op(eng, lambda e: e.tensor_tensor(
        out=u4[:, 0:W], in0=u1[:, 0:W], in1=u4[:, 0:W], op=OP.mult),
        reads=[uq + "u1", uq + "u4"], writes=[uq + "u4"], tag=tag + "ip")
    tr.op(eng, lambda e: e.tensor_tensor(
        out=y[:, 0:W], in0=y[:, 0:W], in1=u4[:, 0:W], op=OP.mult),
        reads=[uq + "y", uq + "u4"], writes=[uq + "y"], tag=tag + "r")
    tr.op(eng, lambda e: e.tensor_scalar(
        out=y[:, 0:W], in0=y[:, 0:W], scalar1=MAGIC, scalar2=MAGIC,
        op0=OP.add, op1=OP.subtract),
        reads=[uq + "y"], writes=[uq + "y"], tag=tag + "rr")
    tr.op(eng, lambda e: e.tensor_scalar(
        out=u2[:, 0:W], in0=u2[:, 0:W], scalar1=1.0, scalar2=0.5,
        op0=OP.add, op1=OP.mult),
        reads=[uq + "u2"], writes=[uq + "u2"], tag=tag + "b1")
    tr.op(eng, lambda e: e.tensor_scalar(
        out=u3[:, 0:W], in0=u3[:, 0:W], scalar1=1.0, scalar2=None, op0=OP.add),
        reads=[uq + "u3"], writes=[uq + "u3"], tag=tag + "b2")
    tr.op(eng, lambda e: e.tensor_tensor(
        out=u2[:, 0:W], in0=u2[:, 0:W], in1=u3[:, 0:W], op=OP.mult),
        reads=[uq + "u2", uq + "u3"], writes=[uq + "u2"], tag=tag + "pq")
    tr.op(eng, lambda e: e.tensor_tensor(
        out=y[:, 0:W], in0=y[:, 0:W], in1=u2[:, 0:W], op=OP.mult),
        reads=[uq + "y", uq + "u2"], writes=[uq + "y"], tag=tag + "qv")
    tr.op(eng, lambda e: e.tensor_tensor(
        out=r3(dst_ap), in0=r3(y[:, 0:W]),
        in1=scl[:, 0:NB].unsqueeze(2).to_broadcast([128, NB, 16]), op=OP.mult),
        reads=[uq + "y", uq + "scl"], writes=[dst_unit], tag=tag + "out")


def mk_fp4_scratch(p, W, name="q4", scoped=False):
    sb = lambda n, sh, dt: p.sbuf(n, sh, dt, scoped=scoped)
    return dict(
        y=sb(name + "y", [128, W], F32), u1=sb(name + "u1", [128, W], BF16),
        u2=sb(name + "u2", [128, W], BF16), u3=sb(name + "u3", [128, W], BF16),
        u4=sb(name + "u4", [128, W], BF16),
        amax=sb(name + "am", [128, W // 16], F32),
        scl=sb(name + "sc", [128, W // 16], F32),
        inv=sb(name + "iv", [128, W // 16], F32), uq=name)



def build_launch_D():
    p = Prog("D")
    nc, tr = p.nc, p.tr
    x_in = p.din("x_in", [TPC, D], F32)
    w_in = {nm: p.din(nm, [D, D], F32) for nm in ("wq", "wk", "wv", "wo")}
    cos_in = p.din("cos_l", [TPC, D], F32)
    sin_in = p.din("sin_l", [TPC, D], F32)
    qT_out = p.dout("qT", [D, TPC], BF16)
    kT_out = p.dout("kT", [D, TPC], BF16)
    v_out = p.dout("v", [TPC, D], BF16)
    woq_out = p.dout("woq", [D, D], BF16)
    w_s = {nm: p.dint(nm + "_s", [D, D], BF16) for nm in ("wq", "wk", "wv")}

    xt = [p.sbuf(f"xt{i}", [128, D], F32, scoped=True) for i in range(2)]
    xlnf = p.sbuf("xlnf", [128, D], F32, scoped=True)
    xq = p.sbuf("xq", [128, NTT * D], BF16, scoped=True)
    xqT = p.sbuf("xqT", [128, NDC * TPC], BF16)
    id_bf = p.sbuf("id_bf", [128, 128], BF16)
    scr = dict(su=p.sbuf("su", [128, 1], F32), negm=p.sbuf("negm", [128, 1], F32),
               ssq=p.sbuf("ssq", [128, 1], F32), std=p.sbuf("std", [128, 1], F32),
               rstd=p.sbuf("rstd", [128, 1], F32),
               sq=p.sbuf("sq", [128, D], BF16, scoped=True),
               squnit="lnscr")
    q4 = mk_fp4_scratch(p, D, scoped=True)
    wchunk = p.sbuf("wchunk", [128, D], F32, scoped=True)
    wqout = p.sbuf("wqout", [128, D], BF16, scoped=True)

    tr.op("gpsimd", lambda e: e.memset(id_bf[:], 0.0), writes=["id"], tag="id0")
    tr.op("gpsimd", lambda e: e.affine_select(
        out=id_bf[:], in_=id_bf[:], compare_op=OP.not_equal, fill=1.0,
        base=0, pattern=[[-1, 128]], channel_multiplier=1),
        reads=["id"], writes=["id"], tag="id1")

    ps_tr = p.psum("ps_tr", [128, 2048], BF16)
    for i in range(NTT):
        b = xt[i % 2]
        tr.dma("sync", (lambda b, i: lambda e: e.dma_start(
            out=b[:], in_=x_in[i * 128:(i + 1) * 128, :]))(b, i),
            writes=[("xt", i % 2)], tag=f"ldx{i}")
        emit_ln(p, ("xt", i % 2), b[:], xlnf[:], ("xlnf",), scr, tag=f"t{i}_")
        emit_fp4(p, "vector", ("xlnf",), xlnf[:], ("xq", i),
                 xq[:, i * D:(i + 1) * D], q4, D, f"qx{i}")
        for half in range(2):
            for jj in range(8):
                j = half * 8 + jj
                tr.op("tensor", (lambda i, jj, j, half: lambda e: e.transpose(
                    ps_tr[:, (half * 8 + jj) * 128:(half * 8 + jj + 1) * 128],
                    xq[:, i * D + j * 128: i * D + (j + 1) * 128], id_bf[:]))(i, jj, j, half),
                    reads=[("xq", i), "id"], writes=[("ptrw", half)],
                    tag=f"tr{i}_{j}")
            for jj in range(8):
                j = half * 8 + jj
                tr.op("vector", (lambda i, jj, j, half: lambda e: e.tensor_copy(
                    xqT[:, j * TPC + i * 128: j * TPC + (i + 1) * 128],
                    ps_tr[:, (half * 8 + jj) * 128:(half * 8 + jj + 1) * 128]))(i, jj, j, half),
                    reads=[("ptrw", half)], writes=[("xqT", i, j)],
                    tag=f"cp{i}_{j}")

    for nm in ("wq", "wk", "wv", "wo"):
        for c in range(NDC):
            eng = "vector"
            tr.dma("sync", (lambda nm, c: lambda e: e.dma_start(
                out=wchunk[:], in_=w_in[nm][c * 128:(c + 1) * 128, :]))(nm, c),
                writes=[("wchunk",)], tag=f"q{nm}{c}ld")
            emit_fp4(p, eng, ("wchunk",), wchunk[:], ("wqout",), wqout[:],
                     q4, D, f"q{nm}{c}")
            dst = woq_out if nm == "wo" else w_s[nm].ap()
            tr.dma("sync", (lambda dst, c: lambda e: e.dma_start(
                out=dst[c * 128:(c + 1) * 128, :], in_=wqout[:]))(dst, c),
                reads=[("wqout",)], writes=[("ws", nm, c)], tag=f"q{nm}{c}st")

    p.new_phase("proj")
    wT = [p.sbuf(f"wTj{i}", [128, D], BF16, scoped=True) for i in range(8)]
    pout = p.sbuf("pout", [128, NTT * D], BF16, scoped=True)
    rope_cos = p.sbuf("rope_cos", [128, D], F32, scoped=True)
    rope_sin = p.sbuf("rope_sin", [128, D], F32, scoped=True)
    tmp1 = p.sbuf("tmp1", [128, D], F32, scoped=True)
    tmp2 = p.sbuf("tmp2", [128, D], F32, scoped=True)
    stg = p.sbuf("stg", [128, TPC], BF16, scoped=True)
    ps_p = p.psum("ps_p", [128, 2048], F32)
    ps_t2 = p.psum("ps_t2", [128, 2048], BF16)

    def proj_mat(nm, src_dram, tagp):
        for half in range(2):
            for jj in range(8):
                j = half * 8 + jj
                tr.dma("scalar", (lambda j, jj, src_dram: lambda e: e.dma_start(
                    out=wT[jj][:], in_=src_dram[:, j * 128:(j + 1) * 128],
                    transpose=True))(j, jj, src_dram),
                    reads=[("ws", nm, cc) for cc in range(NDC)],
                    writes=[("wT", jj)], tag=f"{tagp}td{j}")
            for i in range(NTT):
                po = i % 2
                for o4 in range(4):
                    psl = ps_p[:, (po * 2 + (o4 % 2)) * 512:(po * 2 + (o4 % 2) + 1) * 512]
                    psu = ("psp", po * 2 + (o4 % 2))
                    for jj in range(8):
                        j = half * 8 + jj
                        tr.op("tensor", (lambda i, o4, jj, psl, j: lambda e: e.matmul(
                            psl, xqT[:, j * TPC + i * 128: j * TPC + (i + 1) * 128],
                            wT[jj][:, o4 * 512:(o4 + 1) * 512],
                            start=(jj == 0), stop=(jj == 7)))(i, o4, jj, psl, j),
                            reads=[("wT", jj), ("xqT", i, half * 8 + jj)],
                            writes=[psu], tag=f"{tagp}mm{half}_{i}_{o4}_{jj}")
                    if half == 0:
                        tr.op("vector", (lambda i, o4, psl: lambda e: e.tensor_copy(
                            pout[:, i * D + o4 * 512: i * D + (o4 + 1) * 512], psl))(i, o4, psl),
                            reads=[psu], writes=[("pout", i)], tag=f"{tagp}ev{half}_{i}_{o4}")
                    else:
                        tr.op("vector", (lambda i, o4, psl: lambda e: e.tensor_tensor(
                            out=pout[:, i * D + o4 * 512: i * D + (o4 + 1) * 512],
                            in0=psl, in1=pout[:, i * D + o4 * 512: i * D + (o4 + 1) * 512],
                            op=OP.add))(i, o4, psl),
                            reads=[psu, ("pout", i)], writes=[("pout", i)],
                            tag=f"{tagp}ac{half}_{i}_{o4}")

    def rope_mat(tagp):
        for i in range(NTT):
            tr.dma("sync", (lambda i: lambda e: e.dma_start(
                out=rope_cos[:], in_=cos_in[i * 128:(i + 1) * 128, :]))(i),
                writes=[("rc",)], tag=f"{tagp}lc{i}")
            tr.dma("sync", (lambda i: lambda e: e.dma_start(
                out=rope_sin[:], in_=sin_in[i * 128:(i + 1) * 128, :]))(i),
                writes=[("rs",)], tag=f"{tagp}ls{i}")
            pv = pout[:, i * D:(i + 1) * D]
            r64 = lambda ap: ap.rearrange("p (h k) -> p h k", k=64)
            tr.op("vector", (lambda pv: lambda e: e.tensor_tensor(
                out=tmp1[:], in0=pv, in1=rope_cos[:], op=OP.mult))(pv),
                reads=[("pout", i), ("rc",)], writes=[("tmp1",)], tag=f"{tagp}ra{i}")
            tr.op("vector", (lambda pv: lambda e: e.tensor_tensor(
                out=r64(tmp2[:])[:, :, 0:32], in0=r64(pv)[:, :, 32:64],
                in1=r64(rope_sin[:])[:, :, 0:32], op=OP.mult))(pv),
                reads=[("pout", i), ("rs",)], writes=[("tmp2",)], tag=f"{tagp}rb{i}")
            tr.op("vector", (lambda pv: lambda e: e.tensor_tensor(
                out=r64(tmp2[:])[:, :, 32:64], in0=r64(pv)[:, :, 0:32],
                in1=r64(rope_sin[:])[:, :, 32:64], op=OP.mult))(pv),
                reads=[("pout", i), ("rs",)], writes=[("tmp2",)], tag=f"{tagp}rc{i}")
            tr.op("vector", (lambda pv: lambda e: e.tensor_tensor(
                out=pv, in0=tmp1[:], in1=tmp2[:], op=OP.add))(pv),
                reads=[("tmp1",), ("tmp2",)], writes=[("pout", i)], tag=f"{tagp}rd{i}")

    def store_fmajor(out_hbm, tagp):
        for c in range(NDC):
            for i in range(NTT):
                tr.op("tensor", (lambda c, i: lambda e: e.transpose(
                    ps_t2[:, (i % 16) * 128:(i % 16 + 1) * 128],
                    pout[:, i * D + c * 128: i * D + (c + 1) * 128], id_bf[:]))(c, i),
                    reads=[("pout", i), "id"], writes=[("pt2", (i % 16) // 8)],
                    tag=f"{tagp}tr{c}_{i}")
            for i in range(NTT):
                tr.op("vector", (lambda c, i: lambda e: e.tensor_copy(
                    stg[:, i * 128:(i + 1) * 128],
                    ps_t2[:, (i % 16) * 128:(i % 16 + 1) * 128]))(c, i),
                    reads=[("pt2", (i % 16) // 8)], writes=[("stg", i)], tag=f"{tagp}cp{c}_{i}")
            tr.dma("sync", (lambda c: lambda e: e.dma_start(
                out=out_hbm[c * 128:(c + 1) * 128, :], in_=stg[:]))(c),
                reads=[("stg", i) for i in range(NTT)], tag=f"{tagp}st{c}")

    def store_tmajor(out_hbm, tagp):
        for i in range(NTT):
            tr.dma("sync", (lambda i: lambda e: e.dma_start(
                out=out_hbm[i * 128:(i + 1) * 128, :],
                in_=pout[:, i * D:(i + 1) * D]))(i),
                reads=[("pout", i)], tag=f"{tagp}st{i}")

    proj_mat("wq", w_s["wq"].ap(), "pq")
    rope_mat("rq")
    store_fmajor(qT_out, "sq")
    proj_mat("wk", w_s["wk"].ap(), "pk")
    rope_mat("rk")
    store_fmajor(kT_out, "sk")
    proj_mat("wv", w_s["wv"].ap(), "pv")
    store_tmajor(v_out, "sv")

    return p.finish()


def l1_rope_tables():
    inv_freq = 1.0 / (10000.0 ** (np.arange(0, 32, dtype=np.float64) / 32.0))
    t = np.arange(S, dtype=np.float64)[:, None]
    fr = t * inv_freq[None, :]
    cos1 = np.cos(fr)
    sin1 = np.sin(fr)
    cos_h = np.concatenate([cos1, cos1], axis=1)
    sin_h = np.concatenate([-sin1, sin1], axis=1)
    cos_full = np.tile(cos_h, (1, 32)).astype(np.float32)
    sin_full = np.tile(sin_h, (1, 32)).astype(np.float32)
    return cos_full, sin_full



def build_launch_E():
    p = Prog("E")
    nc, tr = p.nc, p.tr
    x_in = p.din("x_in", [TPC, D], F32)
    kT_ext = p.din("kT_ext", [D, NKT * 128], BF16)
    v_ext = p.din("v_ext", [16, 128, NKT * 130], BF16)
    qT_l = p.din("qT_l", [16, 128, TPC], BF16)
    woq_in = p.din("woq", [D, D], BF16)
    bias_tbl_in = p.din("bias_tbl", [128, NKT], F32)
    scale_tbl_in = p.din("scale_tbl", [128, NKT], F32)
    x_out = p.dout("x_out", [TPC, D], F32)

    o_tok = p.sbuf("o_tok", [128, NTT * D], BF16)
    bias_tbl = p.sbuf("bias_tbl_sb", [128, NKT], F32)
    scale_tbl = p.sbuf("scale_tbl_sb", [128, NKT], F32)
    tri = p.sbuf("tri", [128, 128], F32)
    id_bf = p.sbuf("id_bf", [128, 128], BF16)
    invl = p.sbuf("invl", [128, 1], F32)

    tr.op("gpsimd", lambda e: e.memset(id_bf[:], 0.0), writes=["id"], tag="id0")
    tr.op("gpsimd", lambda e: e.affine_select(
        out=id_bf[:], in_=id_bf[:], compare_op=OP.not_equal, fill=1.0,
        base=0, pattern=[[-1, 128]], channel_multiplier=1),
        reads=["id"], writes=["id"], tag="id1")
    tr.op("gpsimd", lambda e: e.memset(tri[:], 0.0), writes=["tri"], tag="tri0")
    tr.op("gpsimd", lambda e: e.affine_select(
        out=tri[:], in_=tri[:], compare_op=OP.is_ge, fill=-BIG,
        base=0, pattern=[[1, 128]], channel_multiplier=-1),
        reads=["tri"], writes=["tri"], tag="tri1")
    tr.dma("sync", lambda e: e.dma_start(out=bias_tbl[:], in_=bias_tbl_in),
           writes=["btbl"], tag="ld_btbl")
    tr.dma("sync", lambda e: e.dma_start(out=scale_tbl[:], in_=scale_tbl_in),
           writes=["stbl"], tag="ld_stbl")

    kh = [p.sbuf(f"kh{i}", [128, NKT * 128], BF16, scoped=True) for i in range(2)]
    vh = [p.sbuf(f"vh{i}", [128, NKT * 130], BF16, scoped=True) for i in range(2)]
    qh = [p.sbuf(f"qh{i}", [128, TPC], BF16, scoped=True) for i in range(2)]
    ptile = [p.sbuf(f"pt{i}", [128, TPC], BF16, scoped=True) for i in range(3)]
    ctxe = p.sbuf("ctxe", [128, TPC], BF16, scoped=True)
    ps_zz = p.psum("ps_zz", [128, 2048], F32)
    ps_ctx = p.psum("ps_ctx", [128, TPC], F32)
    ps_th = p.psum("ps_th", [128, 1024], BF16)

    for hp in range(16):
        hb = hp % 2
        tr.dma("sync", (lambda hp, hb: lambda e: e.dma_start(
            out=kh[hb][:], in_=kT_ext[hp * 128:(hp + 1) * 128, :]))(hp, hb),
            writes=[("kh", hb)], tag=f"ldk{hp}")
        tr.dma("sync", (lambda hp, hb: lambda e: e.dma_start(
            out=vh[hb][:], in_=v_ext[hp, :, :]))(hp, hb),
            writes=[("vh", hb)], tag=f"ldv{hp}")
        tr.dma("sync", (lambda hp, hb: lambda e: e.dma_start(
            out=qh[hb][:], in_=qT_l[hp, :, :]))(hp, hb),
            writes=[("qh", hb)], tag=f"ldq{hp}")
        for hh in range(2):
            d0 = hh * 64
            for kt in range(NKT):
                zof = (kt % 2) * 1024
                zu = ("zz", kt % 2)
                pt = ptile[kt % 3]
                ptu = ("pt", kt % 3)
                for s in range(2):
                    tr.op("tensor", (lambda hb, kt, s, zof, d0: lambda e: e.matmul(
                        ps_zz[:, zof + s * 512: zof + (s + 1) * 512],
                        kh[hb][d0:d0 + 64, kt * 128:(kt + 1) * 128],
                        qh[hb][d0:d0 + 64, s * 512:(s + 1) * 512],
                        start=True, stop=True))(hb, kt, s, zof, d0),
                        reads=[("kh", hb), ("qh", hb)], writes=[zu],
                        tag=f"z{hp}_{hh}_{kt}_{s}")
                c0 = 0
                if kt >= 64:
                    b = kt - 64
                    c0 = b * 128
                    tr.op("vector", (lambda zof, c0: lambda e: e.tensor_tensor(
                        out=ps_zz[:, zof + c0: zof + c0 + 128],
                        in0=ps_zz[:, zof + c0: zof + c0 + 128], in1=tri[:],
                        op=OP.add))(zof, c0),
                        reads=[zu, "tri"], writes=[zu], tag=f"tri{hp}_{hh}_{kt}")
                tr.op("scalar", (lambda kt, zof, c0, pt: lambda e: e.activation(
                    out=pt[:, c0:TPC], in_=ps_zz[:, zof + c0: zof + TPC], func=AF.Exp,
                    bias=bias_tbl[:, kt:kt + 1],
                    scale=scale_tbl[:, kt:kt + 1]))(kt, zof, c0, pt),
                    reads=[zu, "btbl", "stbl"], writes=[ptu], tag=f"exp{hp}_{hh}_{kt}")
                if c0 > 0:
                    tr.op("vector", (lambda pt, c0: lambda e: e.memset(
                        pt[:, 0:c0], 0.0))(pt, c0),
                        reads=[], writes=[ptu], tag=f"pz{hp}_{hh}_{kt}")
                for s in range(2):
                    tr.op("tensor", (lambda hb, kt, s, pt, hh: lambda e: e.matmul(
                        ps_ctx[0:65, s * 512:(s + 1) * 512],
                        vh[hb][:, kt * 130 + hh * 65: kt * 130 + (hh + 1) * 65],
                        pt[:, s * 512:(s + 1) * 512],
                        start=(kt == 0), stop=(kt == NKT - 1)))(hb, kt, s, pt, hh),
                        reads=[("vh", hb), ptu], writes=[("ctxps",)],
                        tag=f"ctx{hp}_{hh}_{kt}_{s}")
            tr.op("vector", lambda e: e.tensor_copy(ctxe[0:65, :], ps_ctx[0:65, :]),
                  reads=[("ctxps",)], writes=[("ctxe",)], tag=f"ce{hp}_{hh}")
            head = hp * 2 + hh
            for tt in range(NTT):
                tr.op("tensor", (lambda tt: lambda e: e.transpose(
                    ps_th[:, tt * 68: tt * 68 + 65], ctxe[0:65, tt * 128:(tt + 1) * 128],
                    id_bf[0:65, 0:65]))(tt),
                    reads=[("ctxe",), "id"], writes=[("pth",)], tag=f"ct{hp}_{hh}_{tt}")
            for tt in range(NTT):
                tr.op("vector", (lambda tt: lambda e: e.reciprocal(
                    out=invl[:], in_=ps_th[:, tt * 68 + 64: tt * 68 + 65]))(tt),
                    reads=[("pth",)], writes=[("invl",)], tag=f"il{hp}_{hh}_{tt}")
                tr.op("vector", (lambda tt, head: lambda e: e.tensor_scalar(
                    out=o_tok[:, tt * D + head * 64: tt * D + (head + 1) * 64],
                    in0=ps_th[:, tt * 68: tt * 68 + 64], scalar1=invl[:],
                    scalar2=None, op0=OP.mult))(tt, head),
                    reads=[("pth",), ("invl",)], writes=[("otok", tt, head)],
                    tag=f"on{hp}_{hh}_{tt}")

    p.new_phase("oq")
    q4 = mk_fp4_scratch(p, D, name="q4e", scoped=True)
    oq = p.sbuf("oq", [128, D], BF16, scoped=True)
    oqT = p.sbuf("oqT", [128, NDC * TPC], BF16)
    ps_t2 = p.psum("ps_t2", [128, 2048], BF16)
    for i in range(NTT):
        emit_fp4(p, "vector", [("otok", i, hd) for hd in range(32)],
                 o_tok[:, i * D:(i + 1) * D], ("oq",), oq[:], q4, D, f"qo{i}")
        for j in range(NDC):
            tr.op("tensor", (lambda i, j: lambda e: e.transpose(
                ps_t2[:, (j % 16) * 128:(j % 16 + 1) * 128],
                oq[:, j * 128:(j + 1) * 128], id_bf[:]))(i, j),
                reads=[("oq",), "id"], writes=[("pt2", (j % 16) // 8)], tag=f"otr{i}_{j}")
        for j in range(NDC):
            tr.op("vector", (lambda i, j: lambda e: e.tensor_copy(
                oqT[:, j * TPC + i * 128: j * TPC + (i + 1) * 128],
                ps_t2[:, (j % 16) * 128:(j % 16 + 1) * 128]))(i, j),
                reads=[("pt2", (j % 16) // 8)], writes=[("oqT", i, j)], tag=f"ocp{i}_{j}")

    p.new_phase("oproj")
    xt = [p.sbuf(f"xt{i}", [128, D], F32, scoped=True) for i in range(NTT)]
    wT = [p.sbuf(f"wTe{i}", [128, D], BF16, scoped=True) for i in range(8)]
    ps_o = p.psum("ps_o", [128, 2048], F32)
    for i in range(NTT):
        tr.dma("sync", (lambda i: lambda e: e.dma_start(
            out=xt[i][:], in_=x_in[i * 128:(i + 1) * 128, :]))(i),
            writes=[("x", i)], tag=f"ldx{i}")
    for half in range(2):
        for jj in range(8):
            j = half * 8 + jj
            tr.dma("scalar", (lambda j, jj: lambda e: e.dma_start(
                out=wT[jj][:], in_=woq_in[:, j * 128:(j + 1) * 128],
                transpose=True))(j, jj),
                writes=[("wT", jj)], tag=f"otd{j}")
        for i in range(NTT):
            for o4 in range(4):
                slot = (i % 2) * 2 + (o4 % 2)
                psl = ps_o[:, slot * 512:(slot + 1) * 512]
                psu = ("pso", slot)
                for jj in range(8):
                    j = half * 8 + jj
                    tr.op("tensor", (lambda i, o4, jj, psl, j: lambda e: e.matmul(
                        psl, oqT[:, j * TPC + i * 128: j * TPC + (i + 1) * 128],
                        wT[jj][:, o4 * 512:(o4 + 1) * 512],
                        start=(jj == 0), stop=(jj == 7)))(i, o4, jj, psl, j),
                        reads=[("wT", jj), ("oqT", i, j)], writes=[psu],
                        tag=f"om{half}_{i}_{o4}_{jj}")
                tr.op("vector", (lambda i, o4, psl: lambda e: e.tensor_tensor(
                    out=xt[i][:, o4 * 512:(o4 + 1) * 512], in0=psl,
                    in1=xt[i][:, o4 * 512:(o4 + 1) * 512], op=OP.add))(i, o4, psl),
                    reads=[psu, ("x", i)], writes=[("x", i)],
                    tag=f"oa{half}_{i}_{o4}")
    for i in range(NTT):
        tr.dma("sync", (lambda i: lambda e: e.dma_start(
            out=x_out[i * 128:(i + 1) * 128, :], in_=xt[i][:]))(i),
            reads=[("x", i)], tag=f"stx{i}")

    return p.finish()


def l1_attn_tables(c):
    Q0 = c * TPC
    pp = np.arange(128)
    scale = np.zeros((128, NKT), np.float32)
    bias = np.zeros((128, NKT), np.float32)
    for kt in range(NKT):
        valid = True if kt >= 64 else (kt * 128 + 127) < Q0
        scale[:, kt] = 0.125 if valid else 0.0
        bias[:, kt] = -M0 if valid else -BIG
    return scale, bias



MI = 1408
MHC = MI // 128
SHC = 2816 // 128


def build_launch_F():
    p = Prog("F")
    nc, tr = p.nc, p.tr
    x_in = p.din("x_in", [TPC, D], F32)
    gw_in = p.din("gw", [16, D], F32)
    ebias_in = p.din("ebias", [1, 16], F32)
    gp_in = p.din("gp_l", [16, MHC, 128, D], F32)
    up_in = p.din("up_l", [16, MHC, 128, D], F32)
    dp_in = p.din("dp_l", [16, MHC, 128, D], F32)
    sg_in = p.din("sg_l", [SHC, 128, D], F32)
    su_in = p.din("su_l", [SHC, 128, D], F32)
    sd_in = p.din("sd_l", [SHC, 128, D], F32)
    x_out = p.dout("x_out", [TPC, D], F32)

    xt = [p.sbuf(f"xt{i}", [128, D], F32) for i in range(NTT)]
    xln = p.sbuf("xln", [128, D], BF16)
    xlnT = p.sbuf("xlnT", [128, NDC * TPC], BF16)
    id_bf = p.sbuf("id_bf", [128, 128], BF16)
    scr = dict(su=p.sbuf("su", [128, 1], F32), negm=p.sbuf("negm", [128, 1], F32),
               ssq=p.sbuf("ssq", [128, 1], F32), std=p.sbuf("std", [128, 1], F32),
               rstd=p.sbuf("rstd", [128, 1], F32), sq=p.sbuf("sq", [128, D], BF16),
               squnit="lnscr")
    comb_all = p.sbuf("comb_all", [128, NTT * 16], F32)

    tr.op("gpsimd", lambda e: e.memset(id_bf[:], 0.0), writes=["id"], tag="id0")
    tr.op("gpsimd", lambda e: e.affine_select(
        out=id_bf[:], in_=id_bf[:], compare_op=OP.not_equal, fill=1.0,
        base=0, pattern=[[-1, 128]], channel_multiplier=1),
        reads=["id"], writes=["id"], tag="id1")

    ps_tr = p.psum("ps_tr", [128, 2048], BF16)
    emit_ln_and_transpose(p, x_in, xt, xlnT, scr, "id", id_bf, ps_tr, xln, "F")

    gw_sb = p.sbuf("gw_sb", [16, D], BF16, scoped=True)
    gwT = p.sbuf("gwT", [128, NDC * 16], BF16, scoped=True)
    ebias = p.sbuf("ebias_sb", [128, 16], F32, scoped=True)
    sc_sb = p.sbuf("sc_sb", [128, 16], F32, scoped=True)
    s4c = p.sbuf("s4c", [128, 16], F32, scoped=True)
    gs = p.sbuf("gsc", [128, 4], F32, scoped=True)
    gm = p.sbuf("gm", [128, 4], F32, scoped=True)
    tmp6 = p.sbuf("tmp6", [128, 16], F32, scoped=True)
    m6 = p.sbuf("m6", [128, 16], F32, scoped=True)
    wsel = p.sbuf("wsel", [128, 16], F32, scoped=True)
    cnt = p.sbuf("cnt", [128, 16], F32, scoped=True)
    colb = p.sbuf("colb", [128, 1], F32, scoped=True)
    zsum = p.sbuf("zsum", [128, 1], F32, scoped=True)
    t4a = p.sbuf("t4a", [128, 4], F32, scoped=True)
    t4b = p.sbuf("t4b", [128, 4], F32, scoped=True)
    t4c = p.sbuf("t4c", [128, 4], F32, scoped=True)
    t4d = p.sbuf("t4d", [128, 4], F32, scoped=True)
    ps_lg = p.psum("ps_lg", [128, 512], F32)

    tr.dma("gpsimd", lambda e: e.dma_start(out=gw_sb[:], in_=gw_in),
           writes=["gw"], tag="ldgw")
    tr.dma("gpsimd", lambda e: e.dma_start(
        out=ebias[:], in_=ebias_in[0:1, :].to_broadcast([128, 16])),
        writes=["eb"], tag="ldeb")
    for j in range(NDC):
        tr.op("tensor", (lambda j: lambda e: e.transpose(
            ps_tr[:, (j % 16) * 128:(j % 16) * 128 + 16],
            gw_sb[0:16, j * 128:(j + 1) * 128], id_bf[0:16, 0:16]))(j),
            reads=["gw", "id"], writes=[("ptrw", (j % 16) // 8)], tag=f"gwt{j}")
    for j in range(NDC):
        tr.op("vector", (lambda j: lambda e: e.tensor_copy(
            gwT[:, j * 16:(j + 1) * 16],
            ps_tr[:, (j % 16) * 128:(j % 16) * 128 + 16]))(j),
            reads=[("ptrw", (j % 16) // 8)], writes=[("gwT",)], tag=f"gwc{j}")

    for i in range(NTT):
        for j in range(NDC):
            tr.op("tensor", (lambda i, j: lambda e: e.matmul(
                ps_lg[:, 0:16], xlnT[:, j * TPC + i * 128: j * TPC + (i + 1) * 128],
                gwT[:, j * 16:(j + 1) * 16],
                start=(j == 0), stop=(j == NDC - 1)))(i, j),
                reads=[("gwT",), ("xlnT", i, j)], writes=[("pslg",)], tag=f"lg{i}_{j}")
        tr.op("scalar", lambda e: e.activation(
            out=sc_sb[:], in_=ps_lg[:, 0:16], func=AF.Sigmoid),
            reads=[("pslg",)], writes=[("sc",)], tag=f"sig{i}")
        tr.op("vector", lambda e: e.tensor_tensor(
            out=s4c[:], in0=sc_sb[:], in1=ebias[:], op=OP.add),
            reads=[("sc",), "eb"], writes=[("s4c",)], tag=f"s4c{i}")
        v0, v1 = s4c[:].rearrange("p (g j) -> p g j", j=4)[:, :, 0], None
        g4 = lambda j: s4c[:].rearrange("p (g j) -> p g j", j=4)[:, :, j:j + 1].squeeze(2)
        tr.op("vector", lambda e: e.tensor_tensor(out=t4a[:], in0=g4(0), in1=g4(1), op=OP.max),
              reads=[("s4c",)], writes=[("t4a",)], tag=f"g1{i}")
        tr.op("vector", lambda e: e.tensor_tensor(out=t4b[:], in0=g4(0), in1=g4(1), op=OP.min),
              reads=[("s4c",)], writes=[("t4b",)], tag=f"g2{i}")
        tr.op("vector", lambda e: e.tensor_tensor(out=t4c[:], in0=g4(2), in1=g4(3), op=OP.max),
              reads=[("s4c",)], writes=[("t4c",)], tag=f"g3{i}")
        tr.op("vector", lambda e: e.tensor_tensor(out=t4d[:], in0=g4(2), in1=g4(3), op=OP.min),
              reads=[("s4c",)], writes=[("t4d",)], tag=f"g4{i}")
        tr.op("vector", lambda e: e.tensor_tensor(out=gs[:], in0=t4a[:], in1=t4c[:], op=OP.max),
              reads=[("t4a",), ("t4c",)], writes=[("gs",)], tag=f"g5{i}")
        tr.op("vector", lambda e: e.tensor_tensor(out=t4b[:], in0=t4b[:], in1=t4d[:], op=OP.max),
              reads=[("t4b",), ("t4d",)], writes=[("t4b",)], tag=f"g6{i}")
        tr.op("vector", lambda e: e.tensor_tensor(out=t4a[:], in0=t4a[:], in1=t4c[:], op=OP.min),
              reads=[("t4a",), ("t4c",)], writes=[("t4a",)], tag=f"g7{i}")
        tr.op("vector", lambda e: e.tensor_tensor(out=t4a[:], in0=t4a[:], in1=t4b[:], op=OP.max),
              reads=[("t4a",), ("t4b",)], writes=[("t4a",)], tag=f"g8{i}")
        tr.op("vector", lambda e: e.tensor_tensor(out=gs[:], in0=gs[:], in1=t4a[:], op=OP.add),
              reads=[("gs",), ("t4a",)], writes=[("gs",)], tag=f"g9{i}")
        for g in range(4):
            gcol = gs[:, g:g + 1]
            tr.op("vector", (lambda g, gcol: lambda e: e.tensor_tensor(
                out=t4a[:], in0=gs[:], in1=gcol.to_broadcast([128, 4]), op=OP.is_gt))(g, gcol),
                reads=[("gs",)], writes=[("t4a",)], tag=f"c1{i}_{g}")
            if g > 0:
                tr.op("vector", (lambda g, gcol: lambda e: e.tensor_tensor(
                    out=t4b[:, 0:g], in0=gs[:, 0:g], in1=gcol.to_broadcast([128, g]),
                    op=OP.is_equal))(g, gcol),
                    reads=[("gs",)], writes=[("t4b",)], tag=f"c2{i}_{g}")
                tr.op("vector", (lambda g: lambda e: e.tensor_tensor(
                    out=t4a[:, 0:g], in0=t4a[:, 0:g], in1=t4b[:, 0:g], op=OP.add))(g),
                    reads=[("t4a",), ("t4b",)], writes=[("t4a",)], tag=f"c3{i}_{g}")
            tr.op("vector", lambda e: e.tensor_reduce(
                out=colb[:], in_=t4a[:], axis=AX.X, op=OP.add),
                reads=[("t4a",)], writes=[("colb",)], tag=f"c4{i}_{g}")
            tr.op("vector", (lambda g: lambda e: e.tensor_scalar(
                out=gm[:, g:g + 1], in0=colb[:], scalar1=2.0, scalar2=None,
                op0=OP.is_lt))(g),
                reads=[("colb",)], writes=[("gm",)], tag=f"c5{i}_{g}")
        tr.op("vector", lambda e: e.tensor_tensor(
            out=tmp6[:].rearrange("p (g j) -> p g j", j=4),
            in0=s4c[:].rearrange("p (g j) -> p g j", j=4),
            in1=gm[:].unsqueeze(2).to_broadcast([128, 4, 4]), op=OP.mult),
            reads=[("s4c",), ("gm",)], writes=[("tmp6",)], tag=f"te{i}")
        for e_ in range(16):
            ecol = tmp6[:, e_:e_ + 1]
            tr.op("vector", (lambda e_, ecol: lambda e: e.tensor_tensor(
                out=cnt[:], in0=tmp6[:], in1=ecol.to_broadcast([128, 16]),
                op=OP.is_gt))(e_, ecol),
                reads=[("tmp6",)], writes=[("cnt",)], tag=f"t1{i}_{e_}")
            if e_ > 0:
                tr.op("vector", (lambda e_, ecol: lambda e: e.tensor_tensor(
                    out=wsel[:, 0:e_], in0=tmp6[:, 0:e_],
                    in1=ecol.to_broadcast([128, e_]), op=OP.is_equal))(e_, ecol),
                    reads=[("tmp6",)], writes=[("wsel",)], tag=f"t2{i}_{e_}")
                tr.op("vector", (lambda e_: lambda e: e.tensor_tensor(
                    out=cnt[:, 0:e_], in0=cnt[:, 0:e_], in1=wsel[:, 0:e_],
                    op=OP.add))(e_),
                    reads=[("cnt",), ("wsel",)], writes=[("cnt",)], tag=f"t3{i}_{e_}")
            tr.op("vector", lambda e: e.tensor_reduce(
                out=colb[:], in_=cnt[:], axis=AX.X, op=OP.add),
                reads=[("cnt",)], writes=[("colb",)], tag=f"t4{i}_{e_}")
            tr.op("vector", (lambda e_: lambda e: e.tensor_scalar(
                out=m6[:, e_:e_ + 1], in0=colb[:], scalar1=6.0, scalar2=None,
                op0=OP.is_lt))(e_),
                reads=[("colb",)], writes=[("m6",)], tag=f"t5{i}_{e_}")
        tr.op("vector", lambda e: e.tensor_tensor(
            out=wsel[:], in0=m6[:], in1=sc_sb[:], op=OP.mult),
            reads=[("m6",), ("sc",)], writes=[("wsel",)], tag=f"w{i}")
        tr.op("vector", lambda e: e.tensor_reduce(
            out=zsum[:], in_=wsel[:], axis=AX.X, op=OP.add),
            reads=[("wsel",)], writes=[("zsum",)], tag=f"z{i}")
        tr.op("vector", lambda e: e.tensor_scalar(
            out=zsum[:], in0=zsum[:], scalar1=1e-20, scalar2=None, op0=OP.add),
            reads=[("zsum",)], writes=[("zsum",)], tag=f"z2{i}")
        tr.op("vector", lambda e: e.reciprocal(out=zsum[:], in_=zsum[:]),
            reads=[("zsum",)], writes=[("zsum",)], tag=f"z3{i}")
        tr.op("vector", (lambda i: lambda e: e.tensor_scalar(
            out=comb_all[:, i * 16:(i + 1) * 16], in0=wsel[:], scalar1=zsum[:],
            scalar2=2.5, op0=OP.mult, op1=OP.mult))(i),
            reads=[("wsel",), ("zsum",)], writes=[("comb", i)], tag=f"cb{i}")

    p.new_phase("experts")
    gsl = [p.sbuf(f"gsl{i}", [128, D], BF16, scoped=True) for i in range(2)]
    usl = [p.sbuf(f"usl{i}", [128, D], BF16, scoped=True) for i in range(2)]
    dsl = [p.sbuf(f"dsl{i}", [128, D], BF16, scoped=True) for i in range(SHC // 2)]
    inter = p.sbuf("inter", [128, MHC * TPC], BF16, scoped=True)
    gssb = p.sbuf("gssb", [128, TPC], BF16, scoped=True)
    ps_g = p.psum("ps_g", [128, TPC], F32)
    ps_u = p.psum("ps_u", [128, TPC], F32)
    ps_d = p.psum("ps_d", [128, 1024], F32)

    def expert(tagp, nhc, gp_ap, up_ap, dp_ap, comb_col, half_down):
        for hc in range(nhc):
            hb = hc % 2
            tr.dma("gpsimd", (lambda hc, hb, gp_ap: lambda e: e.dma_start(
                out=gsl[hb][:], in_=gp_ap(hc)))(hc, hb, gp_ap),
                writes=[("gsl", hb)], tag=f"{tagp}lg{hc}")
            tr.dma("gpsimd", (lambda hc, hb, up_ap: lambda e: e.dma_start(
                out=usl[hb][:], in_=up_ap(hc)))(hc, hb, up_ap),
                writes=[("usl", hb)], tag=f"{tagp}lu{hc}")
            for s in range(2):
                for j in range(NDC):
                    tr.op("tensor", (lambda hb, s, j: lambda e: e.matmul(
                        ps_g[:, s * 512:(s + 1) * 512], gsl[hb][:, j * 128:(j + 1) * 128],
                        xlnT[:, j * TPC + s * 512: j * TPC + (s + 1) * 512],
                        start=(j == 0), stop=(j == NDC - 1)))(hb, s, j),
                        reads=[("gsl", hb)] + [("xlnT", ii, j) for ii in range(s * 4, s * 4 + 4)],
                        writes=[("psg",)], tag=f"{tagp}gm{hc}_{s}_{j}")
                for j in range(NDC):
                    tr.op("tensor", (lambda hb, s, j: lambda e: e.matmul(
                        ps_u[:, s * 512:(s + 1) * 512], usl[hb][:, j * 128:(j + 1) * 128],
                        xlnT[:, j * TPC + s * 512: j * TPC + (s + 1) * 512],
                        start=(j == 0), stop=(j == NDC - 1)))(hb, s, j),
                        reads=[("usl", hb)] + [("xlnT", ii, j) for ii in range(s * 4, s * 4 + 4)],
                        writes=[("psu",)], tag=f"{tagp}um{hc}_{s}_{j}")
            tr.op("scalar", lambda e: e.activation(
                out=gssb[:], in_=ps_g[:], func=AF.Silu),
                reads=[("psg",)], writes=[("gs2",)], tag=f"{tagp}si{hc}")
            tr.op("vector", (lambda hc: lambda e: e.tensor_tensor(
                out=inter[:, (hc % MHC) * TPC:((hc % MHC) + 1) * TPC], in0=ps_u[:],
                in1=gssb[:], op=OP.mult))(hc),
                reads=[("psu",), ("gs2",)], writes=[("int", hc % MHC)], tag=f"{tagp}it{hc}")
            tr.dma("gpsimd", (lambda hc, dp_ap: lambda e: e.dma_start(
                out=dsl[hc % MHC][:], in_=dp_ap(hc)))(hc, dp_ap),
                writes=[("dsl", hc % MHC)], tag=f"{tagp}ld{hc}")
            if (hc % MHC) == MHC - 1 or hc == nhc - 1:
                n_in = (hc % MHC) + 1
                for i in range(NTT):
                    for c4 in range(4):
                        po = (i * 4 + c4) % 2
                        psl = ps_d[:, po * 512:(po + 1) * 512]
                        psu_ = ("psd", po)
                        for ci in range(n_in):
                            tr.op("tensor", (lambda i, c4, ci, psl: lambda e: e.matmul(
                                psl, inter[:, ci * TPC + i * 128: ci * TPC + (i + 1) * 128],
                                dsl[ci][:, c4 * 512:(c4 + 1) * 512],
                                start=(ci == 0), stop=(ci == n_in - 1)))(i, c4, ci, psl),
                                reads=[("int", ci), ("dsl", ci)], writes=[psu_],
                                tag=f"{tagp}dm{hc}_{i}_{c4}_{ci}")
                        if comb_col is not None:
                            tr.op("vector", (lambda i, c4, psl: lambda e: e.scalar_tensor_tensor(
                                out=xt[i][:, c4 * 512:(c4 + 1) * 512], in0=psl,
                                scalar=comb_col(i), in1=xt[i][:, c4 * 512:(c4 + 1) * 512],
                                op0=OP.mult, op1=OP.add))(i, c4, psl),
                                reads=[psu_, ("x", i)] + comb_reads, writes=[("x", i)],
                                tag=f"{tagp}da{hc}_{i}_{c4}")
                        else:
                            tr.op("vector", (lambda i, c4, psl: lambda e: e.tensor_tensor(
                                out=xt[i][:, c4 * 512:(c4 + 1) * 512], in0=psl,
                                in1=xt[i][:, c4 * 512:(c4 + 1) * 512], op=OP.add))(i, c4, psl),
                                reads=[psu_, ("x", i)], writes=[("x", i)],
                                tag=f"{tagp}da{hc}_{i}_{c4}")

    for ex in range(16):
        comb_reads = [("comb", ii) for ii in range(NTT)]
        expert(f"e{ex}_", MHC,
               (lambda ex: lambda hc: gp_in[ex, hc, :, :])(ex),
               (lambda ex: lambda hc: up_in[ex, hc, :, :])(ex),
               (lambda ex: lambda hc: dp_in[ex, hc, :, :])(ex),
               (lambda ex: lambda i: comb_all[:, i * 16 + ex: i * 16 + ex + 1])(ex),
               False)
    comb_reads = []
    expert("sh_", SHC,
           lambda hc: sg_in[hc, :, :],
           lambda hc: su_in[hc, :, :],
           lambda hc: sd_in[hc, :, :],
           None, True)

    for i in range(NTT):
        tr.dma("sync", (lambda i: lambda e: e.dma_start(
            out=x_out[i * 128:(i + 1) * 128, :], in_=xt[i][:]))(i),
            reads=[("x", i)], tag=f"stx{i}")

    return p.finish()



def _run(nc, in_maps):
    return run_bass_kernel_spmd(nc, in_maps, core_ids=list(range(NCORES))).results


def _attn_ext_layouts(qTs, kTs, vs, nheads_per_tile):
    kT_all = np.concatenate(kTs, axis=1)
    v_all = np.concatenate(vs, axis=0)
    outs = []
    for c in range(NCORES):
        Q0 = c * TPC
        kT_ext = np.ascontiguousarray(
            np.concatenate([kT_all, kT_all[:, Q0:Q0 + TPC]], axis=1))
        vfull = np.concatenate([v_all, v_all[Q0:Q0 + TPC]], axis=0)
        if nheads_per_tile == 1:
            v_ext = np.ascontiguousarray(
                vfull.reshape(NKT, 128, 16, 128).transpose(2, 1, 0, 3)
            ).reshape(16, 128, NKT * 128)
        else:
            vr = vfull.reshape(NKT, 128, 32, 64)
            ones = np.ones((NKT, 128, 32, 1), vfull.dtype)
            vcat = np.concatenate([vr, ones], axis=3).reshape(NKT, 128, 16, 130)
            v_ext = np.ascontiguousarray(vcat.transpose(2, 1, 0, 3)).reshape(
                16, 128, NKT * 130)
        qT_l = np.ascontiguousarray(qTs[c].reshape(16, 128, TPC))
        outs.append((kT_ext, v_ext, qT_l))
    return outs


def kernel(**inputs):
    g = lambda nm: np.asarray(inputs[nm])
    x = g("L0_attn_x")[0]
    xs = [np.ascontiguousarray(x[c * TPC:(c + 1) * TPC]) for c in range(NCORES)]

    nc = build_launch_A()
    res = _run(nc, [{
        "x_in": xs[c], "wq": g("L0_attn_Wq"), "wk": g("L0_attn_Wk"),
        "wv": g("L0_attn_Wv"),
    } for c in range(NCORES)])
    qTs = [r["qT"] for r in res]
    kTs = [r["kT"] for r in res]
    vs = [r["v"] for r in res]

    ext = _attn_ext_layouts(qTs, kTs, vs, 1)
    wo_l = np.ascontiguousarray(
        g("L0_attn_Wo").reshape(16, 128, 16, 128).transpose(2, 1, 0, 3))
    bo_l = np.ascontiguousarray(g("L0_attn_bo").reshape(16, 128).T)
    nc = build_launch_B()
    maps = []
    for c in range(NCORES):
        scale_tbl, bias_tbl = l0_attn_tables(c)
        kT_ext, v_ext, qT_l = ext[c]
        maps.append({"x_in": xs[c], "kT_ext": kT_ext, "v_ext": v_ext,
                     "qT_l": qT_l, "wo_l": wo_l, "bo_l": bo_l,
                     "bias_tbl": bias_tbl, "scale_tbl": scale_tbl})
    res = _run(nc, maps)
    xs = [r["x_out"] for r in res]

    nc = build_launch_C()
    sg_l = np.ascontiguousarray(g("L0_ffn_s_gate").reshape(NHC, 128).T)
    su_l = np.ascontiguousarray(g("L0_ffn_s_up").reshape(NHC, 128).T)
    sd_l = np.ascontiguousarray(g("L0_ffn_s_down").reshape(NDC, 128).T)
    res = _run(nc, [{
        "x_in": xs[c], "wg": g("L0_ffn_wq_gate"), "wu": g("L0_ffn_wq_up"),
        "wd": g("L0_ffn_wq_down"), "sg_l": sg_l, "su_l": su_l, "sd_l": sd_l,
    } for c in range(NCORES)])
    xs = [r["x_out"] for r in res]

    cos_full, sin_full = l1_rope_tables()
    nc = build_launch_D()
    res = _run(nc, [{
        "x_in": xs[c], "wq": g("L1_attn_Wq"), "wk": g("L1_attn_Wk"),
        "wv": g("L1_attn_Wv"), "wo": g("L1_attn_Wo"),
        "cos_l": cos_full[c * TPC:(c + 1) * TPC],
        "sin_l": sin_full[c * TPC:(c + 1) * TPC],
    } for c in range(NCORES)])
    qTs = [r["qT"] for r in res]
    kTs = [r["kT"] for r in res]
    vs = [r["v"] for r in res]
    woq = res[0]["woq"]

    ext = _attn_ext_layouts(qTs, kTs, vs, 2)
    nc = build_launch_E()
    maps = []
    for c in range(NCORES):
        scale_tbl, bias_tbl = l1_attn_tables(c)
        kT_ext, v_ext, qT_l = ext[c]
        maps.append({"x_in": xs[c], "kT_ext": kT_ext, "v_ext": v_ext,
                     "qT_l": qT_l, "woq": woq,
                     "bias_tbl": bias_tbl, "scale_tbl": scale_tbl})
    res = _run(nc, maps)
    xs = [r["x_out"] for r in res]

    def tile_expert(w):
        return np.ascontiguousarray(
            w.T.reshape(16, 128, MHC, 128).transpose(2, 1, 0, 3)).reshape(MHC, 128, D)

    def tile_down(w):
        return np.ascontiguousarray(w.T.reshape(MHC, 128, D))

    gp = g("L1_ffn_gate_proj")
    up = g("L1_ffn_up_proj")
    dp = g("L1_ffn_down_proj")
    gp_l = np.stack([tile_expert(gp[e]) for e in range(16)])
    up_l = np.stack([tile_expert(up[e]) for e in range(16)])
    dp_l = np.stack([tile_down(dp[e]) for e in range(16)])
    sg_l2 = np.ascontiguousarray(
        g("L1_ffn_shared_gate_w").T.reshape(16, 128, SHC, 128).transpose(2, 1, 0, 3)
    ).reshape(SHC, 128, D)
    su_l2 = np.ascontiguousarray(
        g("L1_ffn_shared_up_w").T.reshape(16, 128, SHC, 128).transpose(2, 1, 0, 3)
    ).reshape(SHC, 128, D)
    sd_l2 = np.ascontiguousarray(g("L1_ffn_shared_down_w").T.reshape(SHC, 128, D))
    nc = build_launch_F()
    res = _run(nc, [{
        "x_in": xs[c], "gw": g("L1_ffn_gate_weight"),
        "ebias": g("L1_ffn_e_score_correction_bias").reshape(1, 16),
        "gp_l": gp_l, "up_l": up_l, "dp_l": dp_l,
        "sg_l": sg_l2, "su_l": su_l2, "sd_l": sd_l2,
    } for c in range(NCORES)])
    xs = [r["x_out"] for r in res]

    out = np.concatenate(xs, axis=0)[None, :, :].astype(np.float32)
    return out
